# version 02
# baseline (speedup 1.0000x reference)
import math
from contextlib import ExitStack

import numpy as np
import ml_dtypes

import concourse.bass as bass
import concourse.mybir as mybir
from concourse.bass_utils import run_bass_kernel_spmd

F32 = mybir.dt.float32
I32 = mybir.dt.int32
BF16 = mybir.dt.bfloat16
AF = mybir.ActivationFunctionType
ALU = mybir.AluOpType
AX = mybir.AxisListType

P = 128
D = 1024
FF = 4096
NSTATE = 16
B = 2
L = 8192
NCORES = 8
TC = B * L // NCORES
DBLK = D // P
FBLK = FF // P
TF = 512
NT = TC // TF
EPS = 1e-5


def _gelu64(x):
    x = np.asarray(x, np.float64)
    return 0.5 * x * (1.0 + np.vectorize(math.erf)(x / math.sqrt(2.0)))

def _build_pwl():
    X = np.linspace(-4.0, 4.0, 33)
    Y = _gelu64(X)
    j = np.arange(17)
    qv = 256.0 * Y[16 - j]
    g = (qv[1:] - qv[:-1]) / 64.0
    g0 = g[0]
    dj = g[1:] - g[:-1]
    t = np.arange(-40000, 40001, dtype=np.float64)
    uc = np.minimum(np.abs(t), 1024.0)
    q = g0 * uc
    for k in range(1, 16):
        q = q + dj[k - 1] * np.maximum(uc - 64.0 * k, 0.0)
    mine = np.maximum(t, 0.0) + np.round(q)
    xq = t / 256.0
    y = np.interp(xq, X, Y)
    y = np.where(xq > 4.0, xq, np.where(xq < -4.0, 0.0, y))
    ref = np.round(256.0 * y)
    assert np.array_equal(mine, ref), "PWL hinge table does not match reference"
    return float(g0), [float(v) for v in dj]

_PWL_G0, _PWL_DJ = _build_pwl()


class Buf:
    __slots__ = ("t", "last_write", "readers")
    def __init__(self, t):
        self.t = t
        self.last_write = None
        self.readers = []
    def __getitem__(self, sl):
        return self.t[sl]
    @property
    def ap(self):
        return self.t[:]

SERIALIZE = True
SER_DMA = False
SER_DMA_ST = False
SER_DMA_LD = False
SER_PE = False
P1_NT = None
USE_POOL_ACC = True
SER_ENG = set()
ACT_DRAIN = False
WAIT_MARGIN = 0
PWL_BYPASS = False
LN_BYPASS = False
P1_PROJ = True

class Sch:
    ENGS = ("sync", "vector", "scalar", "gpsimd", "tensor")

    def __init__(self, nc):
        self.nc = nc
        self.ops = {e: [] for e in self.ENGS}
        self.val = {e: 0 for e in self.ENGS}
        self.es = ExitStack()
        self.sem = {e: self.es.enter_context(nc.semaphore(f"s_{e}")) for e in self.ENGS}

    def emit(self, eng, fn, reads=(), writes=(), deps=(), dma=False, psum_src=False):
        dl = list(deps)
        if SERIALIZE or eng in SER_ENG:
            dl.extend((e2, self.val[e2]) for e2 in self.ENGS if self.val[e2] > 0)
        for b in reads:
            if b.last_write is not None:
                dl.append(b.last_write)
        for b in writes:
            if b.last_write is not None:
                dl.append(b.last_write)
            dl.extend(b.readers)
        inc = 16 if dma else 1
        self.val[eng] += inc
        tok = (eng, self.val[eng])
        self.ops[eng].append((fn, dl, inc, psum_src))
        for b in reads:
            b.readers.append(tok)
        for b in writes:
            b.last_write = tok
            b.readers = []
        return tok

    def mm(self, ps, lhsT_buf, lhsT_ap, rhs_buf, rhs_ap, start, stop, out_ap=None):
        dl = []
        if SERIALIZE or SER_PE:
            dl.extend((e2, self.val[e2]) for e2 in self.ENGS if self.val[e2] > 0)
        if lhsT_buf is not None and lhsT_buf.last_write is not None:
            dl.append(lhsT_buf.last_write)
        if rhs_buf is not None and rhs_buf.last_write is not None:
            dl.append(rhs_buf.last_write)
        if start:
            if ps.last_write is not None:
                dl.append(ps.last_write)
            dl.extend(ps.readers)
        self.val["tensor"] += 1
        tok = ("tensor", self.val["tensor"])
        oap = ps.ap if out_ap is None else out_ap
        self.ops["tensor"].append(
            (lambda e, oap=oap, la=lhsT_ap, ra=rhs_ap, st=start, sp=stop:
             e.matmul(oap, la, ra, start=st, stop=sp), dl, 1, False))
        if lhsT_buf is not None:
            lhsT_buf.readers.append(tok)
        if rhs_buf is not None:
            rhs_buf.readers.append(tok)
        if stop:
            ps.last_write = tok
            ps.readers = []
        return tok

    def dma(self, dst_ap, src_ap, reads=(), writes=(), deps=(), eng="sync"):
        dl = list(deps)
        if SER_DMA or (SER_DMA_ST and reads) or (SER_DMA_LD and not reads):
            dl.extend((e2, self.val[e2]) for e2 in self.ENGS if self.val[e2] > 0)
        return self.emit(
            eng, lambda e: e.dma_start(out=dst_ap, in_=src_ap),
            reads=reads, writes=writes, deps=dl, dma=True)

    def finalize(self):
        nc = self.nc
        with nc.Block() as block:
            for eng in self.ENGS:
                if not self.ops[eng]:
                    continue
                def make(eng):
                    def body(e):
                        water = {s: 0 for s in self.ENGS}
                        for fn, deps, inc, psrc in self.ops[eng]:
                            need = {}
                            for (de, dv) in deps:
                                if dv > water[de]:
                                    need[de] = max(need.get(de, 0), dv)
                            waited_pe = False
                            for de, dv in need.items():
                                e.wait_ge(self.sem[de], dv)
                                water[de] = dv
                                if de == "tensor":
                                    waited_pe = True
                            if psrc and waited_pe:
                                e.nop(cycle_cnt=192, nofuse=True)
                            ins = fn(e)
                            ins.then_inc(self.sem[eng], inc)
                    return body
                getattr(block, {"sync": "sync", "vector": "vector", "scalar": "scalar",
                                "gpsimd": "gpsimd", "tensor": "tensor"}[eng])(make(eng))
        self.es.close()


def sbuf(nc, name, shape, dtype):
    return Buf(nc.alloc_sbuf_tensor(name, shape, dtype))

def psum(nc, name, shape, dtype=F32):
    return Buf(nc.alloc_psum_tensor(name, shape, dtype))



def host_fm(x_tok):
    return np.ascontiguousarray(x_tok.T)


def emit_ln(s, C, xblk, xn_out):
    v, a = "vector", "scalar"
    ones_col, ones_row = C["ones_col"], C["ones_row"]
    ps_s1, ps_s2, ps_r, ps_m = C["ps_s1"], C["ps_s2"], C["ps_rstd"], C["ps_mmr"]
    st_mean, st_rstd, st_mmr = C["st_mean"], C["st_rstd"], C["st_mmr"]
    tmp_sq, tmp32 = C["tmp_sq"], C["tmp32"]
    for kb in range(DBLK):
        s.emit(v, lambda e, kb=kb: e.tensor_copy(tmp_sq.ap, xblk[kb].ap),
               reads=[xblk[kb]], writes=[tmp_sq])
        s.mm(ps_s1, ones_col, ones_col.ap, tmp_sq, tmp_sq.ap,
             start=(kb == 0), stop=(kb == DBLK - 1))
        s.emit(a, lambda e, kb=kb: e.activation(C["tmp_sq2"].ap, xblk[kb].ap, AF.Square),
               reads=[xblk[kb]], writes=[C["tmp_sq2"]])
        s.mm(ps_s2, ones_col, ones_col.ap, C["tmp_sq2"], C["tmp_sq2"].ap,
             start=(kb == 0), stop=(kb == DBLK - 1))
    s.emit(v, lambda e: e.tensor_scalar(st_mean.ap, ps_s1.ap, 1.0 / D, None, ALU.mult),
           reads=[ps_s1], writes=[st_mean], psum_src=True)
    s.emit(v, lambda e: e.tensor_mul(st_rstd.ap, st_mean.ap, st_mean.ap),
           reads=[st_mean], writes=[st_rstd])
    s.emit(v, lambda e: e.scalar_tensor_tensor(st_rstd.ap, ps_s2.ap, 1.0 / D,
                                               st_rstd.ap, ALU.mult, ALU.subtract),
           reads=[ps_s2, st_rstd], writes=[st_rstd], psum_src=True)
    s.emit(a, lambda e: e.activation(st_rstd.ap, st_rstd.ap, AF.Ln,
                                     bias=C["eps1"][0:1, 0:1]),
           reads=[st_rstd, C["eps1"]], writes=[st_rstd])
    s.emit(a, lambda e: e.activation(st_rstd.ap, st_rstd.ap, AF.Exp, scale=-0.5),
           reads=[st_rstd], writes=[st_rstd])
    s.emit(v, lambda e: e.tensor_mul(st_mmr.ap, st_mean.ap, st_rstd.ap),
           reads=[st_mean, st_rstd], writes=[st_mmr])
    s.mm(ps_r, ones_row, ones_row.ap, st_rstd, st_rstd.ap, start=True, stop=True)
    s.mm(ps_m, ones_row, ones_row.ap, st_mmr, st_mmr.ap, start=True, stop=True)
    for kb in range(DBLK):
        s.emit(v, lambda e, kb=kb: e.tensor_mul(tmp32.ap, xblk[kb].ap, ps_r.ap),
               reads=[xblk[kb], ps_r], writes=[tmp32], psum_src=True)
        s.emit(v, lambda e, kb=kb: e.tensor_sub(xn_out[kb].ap, tmp32.ap, ps_m.ap),
               reads=[tmp32, ps_m], writes=[xn_out[kb]], psum_src=True)


def emit_pwl(s, nc, ps_h, b1c_ap, b1c_buf, tiles, hb):
    v, a, g = "vector", "scalar", "gpsimd"
    ti, uc, accp, accn, hin = (tiles[k] for k in ("ti", "uc", "accp", "accn", "hin"))
    s.emit(v, lambda e: e.tensor_scalar(ti.ap, ps_h.ap, 256.0, b1c_ap, ALU.mult, ALU.add),
           reads=[ps_h, b1c_buf], writes=[ti], psum_src=True)
    s.emit(v, lambda e: e.scalar_tensor_tensor(uc.ap, ti.ap, -1.0, ti.ap, ALU.mult, ALU.max),
           reads=[ti], writes=[uc])
    s.emit(v, lambda e: e.tensor_scalar(uc.ap, uc.ap, 1024.0, None, ALU.min),
           reads=[uc], writes=[uc])
    s.emit(v, lambda e: e.tensor_scalar(accp.ap, uc.ap, _PWL_G0, None, ALU.mult),
           reads=[uc], writes=[accp])
    s.emit(v, lambda e: e.scalar_tensor_tensor(accp.ap, ti.ap, 0.0, accp.ap, ALU.max, ALU.add),
           reads=[ti, accp], writes=[accp])
    first_neg = True
    for k in range(1, 16):
        dk = _PWL_DJ[k - 1]
        ad = abs(dk)
        s.emit(a, lambda e, ad=ad, k=k: e.activation(
                   hin.ap, uc.ap, AF.Relu,
                   bias=s.C_hingeb[:, k - 1:k], scale=ad),
               reads=[uc, s.C_hingeb_buf], writes=[hin])
        if dk >= 0:
            s.emit(v, lambda e: e.tensor_add(accp.ap, accp.ap, hin.ap),
                   reads=[hin, accp], writes=[accp])
        elif first_neg:
            s.emit(v, lambda e: e.tensor_copy(accn.ap, hin.ap),
                   reads=[hin], writes=[accn])
            first_neg = False
        elif USE_POOL_ACC:
            s.emit(g, lambda e: e.tensor_tensor(accn.ap, accn.ap, hin.ap, ALU.add),
                   reads=[hin, accn], writes=[accn])
        else:
            s.emit(v, lambda e: e.tensor_add(accn.ap, accn.ap, hin.ap),
                   reads=[hin, accn], writes=[accn])
    s.emit(v, lambda e: e.tensor_tensor(ti.ap, accp.ap, accn.ap, ALU.subtract),
           reads=[accp, accn], writes=[ti])
    s.emit(v, lambda e: e.tensor_copy(hb.ap, ti.ap), reads=[ti], writes=[hb])
    return hb


def emit_ffn(s, C, xblk, w2_h):
    v, a, g, t = "vector", "scalar", "gpsimd", "tensor"
    s.C_hingeb = C["hingeb"].t
    s.C_hingeb_buf = C["hingeb"]
    xn = C["xn"]
    if LN_BYPASS:
        for kb in range(DBLK):
            s.emit(v, lambda e, kb=kb: e.tensor_copy(xn[kb].ap, xblk[kb].ap),
                   reads=[xblk[kb]], writes=[xn[kb]])
    else:
        emit_ln(s, C, xblk, xn)
    h1 = C["h1"]
    ps_h = C["ps_h"]
    for m in range(FBLK):
        ph = ps_h[m % 2]
        for kb in range(DBLK):
            s.mm(ph, C["w1b"][kb], C["w1b"][kb][:, m * P:(m + 1) * P],
                 xn[kb], xn[kb].ap, start=(kb == 0), stop=(kb == DBLK - 1))
        tiles = C["pwl"][m % 2]
        if PWL_BYPASS:
            s.emit(v, lambda e, ph=ph, m=m: e.tensor_copy(h1[m].ap, ph.ap),
                   reads=[ph], writes=[h1[m]], psum_src=True)
        else:
            emit_pwl(s, None, ph, C["b1c"][:, m:m + 1], C["b1c_buf"], tiles, h1[m])
    w2r = w2_h.rearrange("(m p) o -> p m o", p=P)
    for mo in range(DBLK):
        w2b = C["w2b"][mo % 2]
        for h in range(2):
            w2s = C["wst"][h]
            hm = FBLK // 2
            s.dma(w2s.ap, w2r[:, h * hm:(h + 1) * hm, mo * P:(mo + 1) * P],
                  writes=[w2s])
            s.emit(v, lambda e, w2b=w2b, w2s=w2s, h=h, hm=hm: e.tensor_copy(
                       w2b[:, h * hm * P:(h + 1) * hm * P], w2s.ap),
                   reads=[w2s], writes=[w2b])
        po = C["ps_o"][mo % 2]
        for m in range(FBLK):
            s.mm(po, w2b, w2b[:, m * P:(m + 1) * P], h1[m], h1[m].ap,
                 start=(m == 0), stop=(m == FBLK - 1))
        t3 = C["t3"][mo % 2]
        s.emit(v, lambda e, t3=t3, po=po, mo=mo: e.tensor_scalar(
                   t3.ap, po.ap, C["b2c"][:, mo:mo + 1], None, ALU.add),
               reads=[po, C["b2c_buf"]], writes=[t3], psum_src=True)
        s.emit(v, lambda e, t3=t3, mo=mo: e.scalar_tensor_tensor(
                   xblk[mo].ap, t3.ap, 1.0 / 512.0, xblk[mo].ap, ALU.mult, ALU.add),
               reads=[t3, xblk[mo]], writes=[xblk[mo]])



def make_common(s, nc):
    C = {}
    C["ones_col"] = sbuf(nc, "ones_col", [P, 1], BF16)
    C["ones_row"] = sbuf(nc, "ones_row", [1, P], F32)
    s.emit("vector", lambda e: e.memset(C["ones_col"].ap, 1.0), writes=[C["ones_col"]])
    s.emit("vector", lambda e: e.memset(C["ones_row"].ap, 1.0), writes=[C["ones_row"]])
    C["ps_s1"] = psum(nc, "ps_s1", [1, TF])
    C["ps_s2"] = psum(nc, "ps_s2", [1, TF])
    C["ps_rstd"] = psum(nc, "ps_rstd", [P, TF])
    C["ps_mmr"] = psum(nc, "ps_mmr", [P, TF])
    C["eps1"] = sbuf(nc, "eps1", [1, 1], F32)
    s.emit("vector", lambda e: e.memset(C["eps1"].ap, EPS), writes=[C["eps1"]])
    C["st_mean"] = sbuf(nc, "st_mean", [1, TF], F32)
    C["st_rstd"] = sbuf(nc, "st_rstd", [1, TF], F32)
    C["st_mmr"] = sbuf(nc, "st_mmr", [1, TF], F32)
    C["tmp_sq"] = sbuf(nc, "tmp_sq", [P, TF], BF16)
    C["tmp_sq2"] = sbuf(nc, "tmp_sq2", [P, TF], BF16)
    C["tmp32"] = sbuf(nc, "tmp32", [P, TF], F32)
    C["xn"] = [sbuf(nc, f"xn{kb}", [P, TF], BF16) for kb in range(DBLK)]
    return C


def make_ffn_bufs(s, nc, C, w1_h, b1_h, b2_h):
    v = "vector"
    C["w1b"] = [sbuf(nc, f"w1b{kb}", [P, FF], BF16) for kb in range(DBLK)]
    C["wst"] = [sbuf(nc, f"wst{i}", [P, FF // 2], F32) for i in range(2)]
    for kb in range(DBLK):
        for h in range(2):
            st = C["wst"][(kb * 2 + h) % 2]
            sl = slice(h * (FF // 2), (h + 1) * (FF // 2))
            s.dma(st.ap, w1_h[kb * P:(kb + 1) * P, sl], writes=[st])
            s.emit(v, lambda e, kb=kb, st=st, sl=sl: e.tensor_copy(
                       C["w1b"][kb][:, sl], st.ap),
                   reads=[st], writes=[C["w1b"][kb]])
    b1r = sbuf(nc, "b1r", [P, FBLK], F32)
    s.dma(b1r.ap, b1_h[:], writes=[b1r])
    b1c = sbuf(nc, "b1c", [P, FBLK], F32)
    s.emit(v, lambda e: e.tensor_scalar(b1c.ap, b1r.ap, 256.0, None, ALU.mult),
           reads=[b1r], writes=[b1c])
    C["b1c"], C["b1c_buf"] = b1c.t, b1c
    b2r = sbuf(nc, "b2r", [P, DBLK], F32)
    s.dma(b2r.ap, b2_h[:], writes=[b2r])
    b2c = sbuf(nc, "b2c", [P, DBLK], F32)
    s.emit(v, lambda e: e.tensor_scalar(b2c.ap, b2r.ap, 256.0, None, ALU.mult),
           reads=[b2r], writes=[b2c])
    C["b2c"], C["b2c_buf"] = b2c.t, b2c
    C["h1"] = [sbuf(nc, f"h1_{m}", [P, TF], BF16) for m in range(FBLK)]
    C["ps_h"] = [psum(nc, f"ps_h{i}", [P, TF]) for i in range(2)]
    C["ps_o"] = [psum(nc, f"ps_o{i}", [P, TF]) for i in range(2)]
    C["w2b"] = [sbuf(nc, f"w2b{i}", [P, FF], BF16) for i in range(2)]
    C["t3"] = [sbuf(nc, f"t3_{i}", [P, TF], I32) for i in range(2)]
    hb = sbuf(nc, "hingeb", [P, 15], F32)
    for k in range(1, 16):
        ad = abs(_PWL_DJ[k - 1])
        s.emit(v, lambda e, k=k, ad=ad: e.memset(hb[:, k - 1:k], -ad * 64.0 * k),
               writes=[hb])
    C["hingeb"] = hb
    C["pwl"] = []
    for i in range(2):
        C["pwl"].append(dict(
            ti=sbuf(nc, f"pw_ti{i}", [P, TF], I32),
            uc=sbuf(nc, f"pw_uc{i}", [P, TF], F32),
            accp=sbuf(nc, f"pw_ap{i}", [P, TF], F32),
            accn=sbuf(nc, f"pw_an{i}", [P, TF], F32),
            hin=sbuf(nc, f"pw_hi{i}", [P, TF], F32),
        ))
    return C


def build_p1():
    nc = bass.Bass("TRN2", target_bir_lowering=False, debug=False)
    x_h = nc.dram_tensor("x", [D, TC], F32, kind="ExternalInput")
    w1_h = nc.dram_tensor("w1", [D, FF], F32, kind="ExternalInput")
    b1_h = nc.dram_tensor("b1", [P, FBLK], F32, kind="ExternalInput")
    w2_h = nc.dram_tensor("w2", [FF, D], F32, kind="ExternalInput")
    b2_h = nc.dram_tensor("b2", [P, DBLK], F32, kind="ExternalInput")
    wz_h = nc.dram_tensor("wz", [D, D], F32, kind="ExternalInput")
    wx_h = nc.dram_tensor("wx", [D, D], F32, kind="ExternalInput")
    wdt_h = nc.dram_tensor("wdt", [D, D], F32, kind="ExternalInput")
    bz_h = nc.dram_tensor("bz", [P, DBLK], F32, kind="ExternalInput")
    bx_h = nc.dram_tensor("bx", [P, DBLK], F32, kind="ExternalInput")
    bdt_h = nc.dram_tensor("bdt", [P, DBLK], F32, kind="ExternalInput")
    wb_h = nc.dram_tensor("wb", [D, NSTATE], F32, kind="ExternalInput")
    wc_h = nc.dram_tensor("wc", [D, NSTATE], F32, kind="ExternalInput")
    bb_h = nc.dram_tensor("bb", [NSTATE, 1], F32, kind="ExternalInput")
    bc_h = nc.dram_tensor("bc", [NSTATE, 1], F32, kind="ExternalInput")
    x2_h = nc.dram_tensor("x2", [D, TC], F32, kind="ExternalOutput")
    z_h = nc.dram_tensor("z", [D, TC], BF16, kind="ExternalOutput")
    dt_h = nc.dram_tensor("dt", [D, TC], F32, kind="ExternalOutput")
    vv_h = nc.dram_tensor("vv", [D, TC], F32, kind="ExternalOutput")
    bm_h = nc.dram_tensor("bm", [NSTATE, TC], F32, kind="ExternalOutput")
    cc_h = nc.dram_tensor("cc", [NSTATE, TC], F32, kind="ExternalOutput")

    s = Sch(nc)
    v, a, g, t = "vector", "scalar", "gpsimd", "tensor"
    C = make_common(s, nc)
    C = make_ffn_bufs(s, nc, C, w1_h, b1_h, b2_h)
    x2_d = Buf(x2_h)

    xs = [sbuf(nc, f"xs{kb}", [P, TF], F32) for kb in range(DBLK)]

    for f in range(NT if P1_NT is None else P1_NT):
        sl = slice(f * TF, (f + 1) * TF)
        for kb in range(DBLK):
            s.dma(xs[kb].ap, x_h[kb * P:(kb + 1) * P, sl], writes=[xs[kb]])
        emit_ffn(s, C, xs, w2_h)
        for kb in range(DBLK):
            s.dma(x2_h[kb * P:(kb + 1) * P, sl], xs[kb].ap, reads=[xs[kb]], writes=[x2_d])

    pbias = {}
    for name, h in (("bz", bz_h), ("bx", bx_h), ("bdt", bdt_h)):
        bb = sbuf(nc, f"{name}c", [P, DBLK], F32)
        s.dma(bb.ap, h[:], writes=[bb])
        pbias[name] = bb
    bb16 = sbuf(nc, "bbc", [NSTATE, 1], F32)
    s.dma(bb16.ap, bb_h[:], writes=[bb16])
    bc16 = sbuf(nc, "bcc", [NSTATE, 1], F32)
    s.dma(bc16.ap, bc_h[:], writes=[bc16])

    wofs = {"wz": 0, "wx": D, "wdt": 2 * D}
    wres = {}
    for name, h in (("wz", wz_h), ("wx", wx_h), ("wdt", wdt_h)):
        off = wofs[name]
        for kb in range(DBLK):
            st = C["wst"][kb % 2]
            s.dma(st[:, 0:D], h[kb * P:(kb + 1) * P, :], writes=[st])
            s.emit(v, lambda e, kb=kb, st=st, off=off: e.tensor_copy(
                       C["w1b"][kb][:, off:off + D], st[:, 0:D]),
                   reads=[st], writes=[C["w1b"][kb]])
        wres[name] = [(C["w1b"][kb], off) for kb in range(DBLK)]
    for name, h in (("wb", wb_h), ("wc", wc_h)):
        st = C["wst"][0]
        s.dma(st[:, 0:DBLK * NSTATE],
              h.rearrange("(k p) n -> p k n", p=P), writes=[st])
        wb = sbuf(nc, f"{name}b", [P, DBLK * NSTATE], BF16)
        s.emit(v, lambda e, wb=wb, st=st: e.tensor_copy(wb.ap, st[:, 0:DBLK * NSTATE]),
               reads=[st], writes=[wb])
        wres[name] = wb

    xssm = [sbuf(nc, f"xssm{i}", [P, TF], F32) for i in range(2)]
    dtt = [sbuf(nc, f"dtt{i}", [P, TF], F32) for i in range(2)]
    pb = [sbuf(nc, f"pb{i}", [P, TF], F32) for i in range(2)]
    sc1 = [sbuf(nc, f"sc1_{i}", [P, TF], F32) for i in range(2)]
    zt = sbuf(nc, "zt", [P, TF], BF16)
    ps_bc = C["ps_rstd"]
    sm16 = sbuf(nc, "sm16", [NSTATE, TF], F32)

    for f in range((NT if P1_NT is None else P1_NT) if P1_PROJ else 0):
        sl = slice(f * TF, (f + 1) * TF)
        for kb in range(DBLK):
            s.dma(xs[kb].ap, x2_h[kb * P:(kb + 1) * P, sl], reads=[x2_d], writes=[xs[kb]])
        xn = C["xn"]
        emit_ln(s, C, xs, xn)
        for mo in range(DBLK):
            po = C["ps_o"][mo % 2]
            for kb in range(DBLK):
                wbuf, off = wres["wz"][kb]
                s.mm(po, wbuf, wbuf[:, off + mo * P:off + (mo + 1) * P],
                     xn[kb], xn[kb].ap, start=(kb == 0), stop=(kb == DBLK - 1))
            s.emit(a, lambda e, po=po, mo=mo: e.activation(
                       zt.ap, po.ap, AF.Sigmoid, bias=pbias["bz"][:, mo:mo + 1]),
                   reads=[po, pbias["bz"]], writes=[zt], psum_src=True)
            s.dma(z_h[mo * P:(mo + 1) * P, sl], zt.ap, reads=[zt])
        for mo in range(DBLK):
            po = C["ps_o"][mo % 2]
            for kb in range(DBLK):
                wbuf, off = wres["wdt"][kb]
                s.mm(po, wbuf, wbuf[:, off + mo * P:off + (mo + 1) * P],
                     xn[kb], xn[kb].ap, start=(kb == 0), stop=(kb == DBLK - 1))
            i2 = mo % 2
            s.emit(a, lambda e, po=po, mo=mo, i2=i2: e.activation(
                       pb[i2].ap, po.ap, AF.Identity, bias=pbias["bdt"][:, mo:mo + 1]),
                   reads=[po, pbias["bdt"]], writes=[pb[i2]], psum_src=True)
            s.emit(v, lambda e, i2=i2: e.scalar_tensor_tensor(
                       sc1[i2].ap, pb[i2].ap, -1.0, pb[i2].ap, ALU.mult, ALU.max),
                   reads=[pb[i2]], writes=[sc1[i2]])
            s.emit(a, lambda e, i2=i2: e.activation(sc1[i2].ap, sc1[i2].ap, AF.Exp,
                                                    scale=-1.0),
                   reads=[sc1[i2]], writes=[sc1[i2]])
            s.emit(a, lambda e, i2=i2: e.activation(sc1[i2].ap, sc1[i2].ap, AF.Ln,
                                                    bias=1.0),
                   reads=[sc1[i2]], writes=[sc1[i2]])
            s.emit(v, lambda e, i2=i2: e.scalar_tensor_tensor(
                       dtt[i2].ap, pb[i2].ap, 0.0, sc1[i2].ap, ALU.max, ALU.add),
                   reads=[pb[i2], sc1[i2]], writes=[dtt[i2]])
            s.dma(dt_h[mo * P:(mo + 1) * P, sl], dtt[i2].ap, reads=[dtt[i2]])
            po2 = C["ps_h"][mo % 2]
            for kb in range(DBLK):
                wbuf, off = wres["wx"][kb]
                s.mm(po2, wbuf, wbuf[:, off + mo * P:off + (mo + 1) * P],
                     xn[kb], xn[kb].ap, start=(kb == 0), stop=(kb == DBLK - 1))
            s.emit(a, lambda e, po2=po2, mo=mo, i2=i2: e.activation(
                       xssm[i2].ap, po2.ap, AF.Identity, bias=pbias["bx"][:, mo:mo + 1]),
                   reads=[po2, pbias["bx"]], writes=[xssm[i2]], psum_src=True)
            s.emit(v, lambda e, i2=i2: e.tensor_mul(xssm[i2].ap, xssm[i2].ap, dtt[i2].ap),
                   reads=[dtt[i2], xssm[i2]], writes=[xssm[i2]])
            s.dma(vv_h[mo * P:(mo + 1) * P, sl], xssm[i2].ap, reads=[xssm[i2]])
        for name, out_h, bias in (("wb", bm_h, bb16), ("wc", cc_h, bc16)):
            for kb in range(DBLK):
                s.mm(ps_bc, wres[name], wres[name][:, kb * NSTATE:(kb + 1) * NSTATE],
                     xn[kb], xn[kb].ap, start=(kb == 0), stop=(kb == DBLK - 1),
                     out_ap=ps_bc.t[0:NSTATE, :])
            s.emit(a, lambda e, bias=bias: e.activation(
                       sm16.ap, ps_bc.t[0:NSTATE, :], AF.Identity, bias=bias[:, 0:1]),
                   reads=[ps_bc, bias], writes=[sm16], psum_src=True)
            s.dma(out_h[:, sl], sm16.ap, reads=[sm16])

    s.emit("sync", lambda e: e.nop(), deps=[("sync", s.val["sync"])])
    s.finalize()
    return nc



def build_p2():
    nc = bass.Bass("TRN2", target_bir_lowering=False, debug=False)
    TP2 = 512
    NCH = L // TP2
    dt_h = [nc.dram_tensor(f"dt{b}", [P, L], F32, kind="ExternalInput") for b in range(B)]
    vv_h = [nc.dram_tensor(f"vv{b}", [P, L], F32, kind="ExternalInput") for b in range(B)]
    bm_h = [nc.dram_tensor(f"bm{b}", [NSTATE, L], F32, kind="ExternalInput") for b in range(B)]
    cc_h = [nc.dram_tensor(f"cm{b}", [NSTATE, L], F32, kind="ExternalInput") for b in range(B)]
    al_h = nc.dram_tensor("alog", [P, NSTATE], F32, kind="ExternalInput")
    y_h = [nc.dram_tensor(f"y{b}", [P, L], BF16, kind="ExternalOutput") for b in range(B)]

    s = Sch(nc)
    v, a, g, t = "vector", "scalar", "gpsimd", "tensor"

    dtb = sbuf(nc, "dtb", [P, L], F32)
    vb = sbuf(nc, "vb", [P, L], F32)
    acol = sbuf(nc, "acol", [P, NSTATE], F32)
    onesf = sbuf(nc, "onesf", [1, P], F32)
    rowB = [sbuf(nc, f"rowB{i}", [1, TP2], F32) for i in range(2)]
    rowC = [sbuf(nc, f"rowC{i}", [1, TP2], F32) for i in range(2)]
    ps_B = [psum(nc, f"ps_B{i}", [P, TP2]) for i in range(2)]
    ps_C = [psum(nc, f"ps_C{i}", [P, TP2]) for i in range(2)]
    at = [sbuf(nc, f"at{i}", [P, TP2], F32) for i in range(2)]
    ut = [sbuf(nc, f"ut{i}", [P, TP2], F32) for i in range(2)]
    ht = [sbuf(nc, f"ht{i}", [P, TP2], F32) for i in range(2)]
    q = sbuf(nc, "q", [P, TP2, NSTATE], F32)
    y32 = sbuf(nc, "y32", [P, TP2], F32)
    yb = sbuf(nc, "yb", [P, TP2], BF16)
    carry = sbuf(nc, "carry", [P, NSTATE], F32)

    s.emit(v, lambda e: e.memset(onesf.ap, 1.0), writes=[onesf])
    s.dma(acol.ap, al_h[:], writes=[acol])
    s.emit(a, lambda e: e.activation(acol.ap, acol.ap, AF.Exp), reads=[acol], writes=[acol])
    s.emit(v, lambda e: e.tensor_scalar(acol.ap, acol.ap, -1.0, None, ALU.mult),
           reads=[acol], writes=[acol])

    for b in range(B):
        s.dma(dtb.ap, dt_h[b][:], writes=[dtb])
        s.dma(vb.ap, vv_h[b][:], writes=[vb])
        s.emit(v, lambda e: e.memset(carry.ap, 0.0), writes=[carry])
        for c in range(NCH):
            sl = slice(c * TP2, (c + 1) * TP2)
            for n in range(NSTATE):
                i2 = n % 2
                rB, rC, pB, pC = rowB[i2], rowC[i2], ps_B[i2], ps_C[i2]
                s.dma(rB.ap, bm_h[b][n:n + 1, sl], writes=[rB])
                s.dma(rC.ap, cc_h[b][n:n + 1, sl], writes=[rC])
                s.mm(pB, onesf, onesf.ap, rB, rB.ap, start=True, stop=True)
                s.mm(pC, onesf, onesf.ap, rC, rC.ap, start=True, stop=True)
                s.emit(a, lambda e, i2=i2, n=n, sl=sl: e.activation(
                           at[i2].ap, dtb[:, sl], AF.Exp, scale=acol[:, n:n + 1]),
                       reads=[dtb, acol], writes=[at[i2]])
                s.emit(v, lambda e, i2=i2, sl=sl: e.tensor_tensor(
                           ut[i2].ap, vb[:, sl], ps_B[i2].ap, ALU.mult),
                       reads=[vb, ps_B[i2]], writes=[ut[i2]], psum_src=True)
                s.emit(v, lambda e, i2=i2, n=n: e.tensor_tensor_scan(
                           ht[i2].ap, at[i2].ap, ut[i2].ap, carry[:, n:n + 1],
                           ALU.mult, ALU.add),
                       reads=[at[i2], ut[i2], carry], writes=[ht[i2]])
                s.emit(v, lambda e, i2=i2, n=n: e.tensor_copy(
                           carry[:, n:n + 1], ht[i2][:, TP2 - 1:TP2]),
                       reads=[ht[i2]], writes=[carry])
                s.emit(v, lambda e, i2=i2, n=n: e.tensor_tensor(
                           q.t[:, :, n], ht[i2].ap, ps_C[i2].ap, ALU.mult),
                       reads=[ht[i2], ps_C[i2]], writes=([q] if n == NSTATE - 1 else []),
                       deps=([q.last_write] if (n == 0 and q.last_write) else []),
                       psum_src=True)
            s.emit(v, lambda e: e.tensor_reduce(y32.ap, q.ap, AX.X, ALU.add),
                   reads=[q], writes=[y32])
            s.emit(v, lambda e: e.tensor_copy(yb.ap, y32.ap), reads=[y32], writes=[yb])
            s.dma(y_h[b][:, sl], yb.ap, reads=[yb])
    s.emit("sync", lambda e: e.nop(), deps=[("sync", s.val["sync"])])
    s.finalize()
    return nc


def p2_inmaps(p1res):
    dt_all = [np.asarray(r["dt"], np.float32) for r in p1res]
    vv_all = [np.asarray(r["vv"], np.float32) for r in p1res]
    bm_all = [np.asarray(r["bm"], np.float32) for r in p1res]
    cc_all = [np.asarray(r["cc"], np.float32) for r in p1res]
    alog = p1res[0]["_alog"]
    maps = []
    for j in range(NCORES):
        m = {}
        for b in range(B):
            cores = range(b * 4, b * 4 + 4)
            m[f"dt{b}"] = np.ascontiguousarray(
                np.concatenate([dt_all[i][j * P:(j + 1) * P, :] for i in cores], axis=1))
            m[f"vv{b}"] = np.ascontiguousarray(
                np.concatenate([vv_all[i][j * P:(j + 1) * P, :] for i in cores], axis=1))
            m[f"bm{b}"] = np.ascontiguousarray(
                np.concatenate([bm_all[i] for i in cores], axis=1))
            m[f"cm{b}"] = np.ascontiguousarray(
                np.concatenate([cc_all[i] for i in cores], axis=1))
        m["alog"] = np.ascontiguousarray(alog[j * P:(j + 1) * P, :])
        maps.append(m)
    return maps



def build_p3():
    nc = bass.Bass("TRN2", target_bir_lowering=False, debug=False)
    x2_h = nc.dram_tensor("x2", [D, TC], F32, kind="ExternalInput")
    y_h = nc.dram_tensor("y", [D, TC], BF16, kind="ExternalInput")
    z_h = nc.dram_tensor("z", [D, TC], BF16, kind="ExternalInput")
    wo_h = nc.dram_tensor("wo", [D, D], F32, kind="ExternalInput")
    bo_h = nc.dram_tensor("bo", [P, DBLK], F32, kind="ExternalInput")
    w1_h = nc.dram_tensor("w1", [D, FF], F32, kind="ExternalInput")
    b1_h = nc.dram_tensor("b1", [P, FBLK], F32, kind="ExternalInput")
    w2_h = nc.dram_tensor("w2", [FF, D], F32, kind="ExternalInput")
    b2_h = nc.dram_tensor("b2", [P, DBLK], F32, kind="ExternalInput")
    xo_h = nc.dram_tensor("xo", [D, TC], F32, kind="ExternalOutput")

    s = Sch(nc)
    v, a, g, t = "vector", "scalar", "gpsimd", "tensor"
    C = make_common(s, nc)
    C = make_ffn_bufs(s, nc, C, w1_h, b1_h, b2_h)

    boc = sbuf(nc, "boc", [P, DBLK], F32)
    s.dma(boc.ap, bo_h[:], writes=[boc])

    xs = [sbuf(nc, f"xs{kb}", [P, TF], F32) for kb in range(DBLK)]
    gt = C["h1"][:DBLK]
    zt = sbuf(nc, "zt3", [P, TF], BF16)
    yt = sbuf(nc, "yt3", [P, TF], BF16)
    xo_d = Buf(xo_h)

    for f in range(NT):
        sl = slice(f * TF, (f + 1) * TF)
        for kb in range(DBLK):
            s.dma(xs[kb].ap, x2_h[kb * P:(kb + 1) * P, sl], writes=[xs[kb]])
        wob = C["w2b"][0]
        for kb in range(DBLK):
            st = C["wst"][kb % 2]
            s.dma(st[:, 0:D], wo_h[kb * P:(kb + 1) * P, :], writes=[st])
            s.emit(v, lambda e, kb=kb, st=st, wob=wob: e.tensor_copy(
                       wob[:, kb * 512:kb * 512 + 512], st[:, 0:512]),
                   reads=[st], writes=[wob])
            s.emit(v, lambda e, kb=kb, st=st, wob=wob: e.tensor_copy(
                       C["w2b"][1][:, kb * 512:kb * 512 + 512], st[:, 512:D]),
                   reads=[st], writes=[C["w2b"][1]])
        for kb in range(DBLK):
            s.dma(yt.ap, y_h[kb * P:(kb + 1) * P, sl], writes=[yt])
            s.dma(zt.ap, z_h[kb * P:(kb + 1) * P, sl], writes=[zt])
            s.emit(v, lambda e, kb=kb: e.tensor_tensor(gt[kb].ap, yt.ap, zt.ap, ALU.mult),
                   reads=[yt, zt], writes=[gt[kb]])
        for mo in range(DBLK):
            po = C["ps_o"][mo % 2]
            for kb in range(DBLK):
                half, off = divmod(mo * P, 512)
                wob2 = C["w2b"][half]
                s.mm(po, wob2, wob2[:, kb * 512 + off:kb * 512 + off + P],
                     gt[kb], gt[kb].ap, start=(kb == 0), stop=(kb == DBLK - 1))
            s.emit(v, lambda e, po=po, mo=mo: e.scalar_tensor_tensor(
                       xs[mo].ap, po.ap, boc[:, mo:mo + 1], xs[mo].ap, ALU.add, ALU.add),
                   reads=[po, boc, xs[mo]], writes=[xs[mo]], psum_src=True)
        emit_ffn(s, C, xs, w2_h)
        for kb in range(DBLK):
            s.dma(xo_h[kb * P:(kb + 1) * P, sl], xs[kb].ap, reads=[xs[kb]], writes=[xo_d])
    s.emit("sync", lambda e: e.nop(), deps=[("sync", s.val["sync"])])
    s.finalize()
    return nc


def p3_inmaps(inp, p1res, p2res):
    maps = []
    shared = dict(
        wo=np.ascontiguousarray(np.asarray(inp["L0_attn_Wout"], np.float32)),
        bo=_col(np.asarray(inp["L0_attn_bout"], np.float32)),
        w1=np.ascontiguousarray(np.asarray(inp["L0_ffn_w1"], np.float32)),
        b1=_col(np.asarray(inp["L0_ffn_b1"], np.float32)),
        w2=np.ascontiguousarray(np.asarray(inp["L0_ffn_w2"], np.float32)),
        b2=_col(np.asarray(inp["L0_ffn_b2"], np.float32)),
    )
    for i in range(NCORES):
        b, qq = divmod(i, 4)
        ysl = slice(qq * TC, (qq + 1) * TC)
        yfm = np.concatenate([np.asarray(p2res[j][f"y{b}"])[:, ysl] for j in range(NCORES)], axis=0)
        m = dict(shared)
        m["x2"] = np.asarray(p1res[i]["x2"], np.float32)
        m["z"] = np.asarray(p1res[i]["z"])
        m["y"] = np.ascontiguousarray(yfm)
        maps.append(m)
    return maps


def kernel(**inputs):
    nc1 = build_p1()
    maps1 = p1_inmaps(inputs)
    r1 = run_bass_kernel_spmd(nc1, maps1, core_ids=list(range(NCORES)))
    p1res = [dict(r) for r in r1.results]
    for r in p1res:
        r["_alog"] = np.asarray(inputs["L0_attn_A_log"], np.float32)

    nc2 = build_p2()
    maps2 = p2_inmaps(p1res)
    r2 = run_bass_kernel_spmd(nc2, maps2, core_ids=list(range(NCORES)))
    p2res = [dict(r) for r in r2.results]

    nc3 = build_p3()
    maps3 = p3_inmaps(inputs, p1res, p2res)
    r3 = run_bass_kernel_spmd(nc3, maps3, core_ids=list(range(NCORES)))

    out = np.empty((B * L, D), np.float32)
    for i in range(NCORES):
        out[i * TC:(i + 1) * TC, :] = np.asarray(r3.results[i]["xo"], np.float32).T
    return out.reshape(B, L, D)


def _col(b):
    n = b.shape[0] // P
    return np.ascontiguousarray(b.reshape(n, P).T)


def p1_inmaps(inp):
    xf = np.asarray(inp["L0_attn_x"], np.float32).reshape(B * L, D)
    shared = dict(
        w1=np.ascontiguousarray(np.asarray(inp["L0_ffn_w1"], np.float32)),
        b1=_col(np.asarray(inp["L0_ffn_b1"], np.float32)),
        w2=np.ascontiguousarray(np.asarray(inp["L0_ffn_w2"], np.float32)),
        b2=_col(np.asarray(inp["L0_ffn_b2"], np.float32)),
        wz=np.ascontiguousarray(np.asarray(inp["L0_attn_Wz"], np.float32)),
        wx=np.ascontiguousarray(np.asarray(inp["L0_attn_Wx"], np.float32)),
        wdt=np.ascontiguousarray(np.asarray(inp["L0_attn_Wdt"], np.float32)),
        bz=_col(np.asarray(inp["L0_attn_bz"], np.float32)),
        bx=_col(np.asarray(inp["L0_attn_bx"], np.float32)),
        bdt=_col(np.asarray(inp["L0_attn_bdt"], np.float32)),
        wb=np.ascontiguousarray(np.asarray(inp["L0_attn_WB"], np.float32)),
        wc=np.ascontiguousarray(np.asarray(inp["L0_attn_WC"], np.float32)),
        bb=np.ascontiguousarray(np.asarray(inp["L0_attn_bB"], np.float32).reshape(NSTATE, 1)),
        bc=np.ascontiguousarray(np.asarray(inp["L0_attn_bC"], np.float32).reshape(NSTATE, 1)),
    )
    maps = []
    for i in range(NCORES):
        m = dict(shared)
        m["x"] = np.ascontiguousarray(xf[i * TC:(i + 1) * TC, :].T)
        maps.append(m)
    return maps


# version 03
# speedup vs baseline: 1.2506x; 1.2506x over previous
import math
from contextlib import ExitStack

import numpy as np
import ml_dtypes

import concourse.bass as bass
import concourse.mybir as mybir
from concourse.bass_utils import run_bass_kernel_spmd

F32 = mybir.dt.float32
I32 = mybir.dt.int32
BF16 = mybir.dt.bfloat16
AF = mybir.ActivationFunctionType
ALU = mybir.AluOpType
AX = mybir.AxisListType

P = 128
D = 1024
FF = 4096
NSTATE = 16
B = 2
L = 8192
NCORES = 8
TC = B * L // NCORES
DBLK = D // P
FBLK = FF // P
TF = 512
NT = TC // TF
EPS = 1e-5


def _gelu64(x):
    x = np.asarray(x, np.float64)
    return 0.5 * x * (1.0 + np.vectorize(math.erf)(x / math.sqrt(2.0)))

def _build_pwl():
    X = np.linspace(-4.0, 4.0, 33)
    Y = _gelu64(X)
    j = np.arange(17)
    qv = 256.0 * Y[16 - j]
    g = (qv[1:] - qv[:-1]) / 64.0
    g0 = g[0]
    dj = g[1:] - g[:-1]
    t = np.arange(-40000, 40001, dtype=np.float64)
    uc = np.minimum(np.abs(t), 1024.0)
    q = g0 * uc
    for k in range(1, 16):
        q = q + dj[k - 1] * np.maximum(uc - 64.0 * k, 0.0)
    mine = np.maximum(t, 0.0) + np.round(q)
    xq = t / 256.0
    y = np.interp(xq, X, Y)
    y = np.where(xq > 4.0, xq, np.where(xq < -4.0, 0.0, y))
    ref = np.round(256.0 * y)
    assert np.array_equal(mine, ref), "PWL hinge table does not match reference"
    return float(g0), [float(v) for v in dj]

_PWL_G0, _PWL_DJ = _build_pwl()


class Buf:
    __slots__ = ("t", "last_write", "readers")
    def __init__(self, t):
        self.t = t
        self.last_write = None
        self.readers = []
    def __getitem__(self, sl):
        return self.t[sl]
    @property
    def ap(self):
        return self.t[:]

SERIALIZE = True
SER_DMA = False
SER_DMA_ST = False
SER_DMA_LD = False
SER_PE = False
P1_NT = None
USE_POOL_ACC = True
SER_ENG = set()
ACT_DRAIN = False
WAIT_MARGIN = 0
PWL_BYPASS = False
LN_BYPASS = False
P1_PROJ = True

class Sch:
    ENGS = ("sync", "vector", "scalar", "gpsimd", "tensor")

    def __init__(self, nc):
        self.nc = nc
        self.ops = {e: [] for e in self.ENGS}
        self.val = {e: 0 for e in self.ENGS}
        self.es = ExitStack()
        self.sem = {e: self.es.enter_context(nc.semaphore(f"s_{e}")) for e in self.ENGS}
        self.prev = None

    def emit(self, eng, fn, reads=(), writes=(), deps=(), dma=False, psum_src=False):
        dl = list(deps)
        if SERIALIZE or eng in SER_ENG:
            if self.prev is not None and self.prev[0] != eng:
                dl.append(self.prev)
        for b in reads:
            if b.last_write is not None:
                dl.append(b.last_write)
        for b in writes:
            if b.last_write is not None:
                dl.append(b.last_write)
            dl.extend(b.readers)
        inc = 16 if dma else 1
        self.val[eng] += inc
        tok = (eng, self.val[eng])
        self.ops[eng].append((fn, dl, inc, psum_src))
        self.prev = tok
        for b in reads:
            b.readers.append(tok)
        for b in writes:
            b.last_write = tok
            b.readers = []
        return tok

    def mm(self, ps, lhsT_buf, lhsT_ap, rhs_buf, rhs_ap, start, stop, out_ap=None):
        dl = []
        if SERIALIZE or SER_PE:
            if self.prev is not None and self.prev[0] != "tensor":
                dl.append(self.prev)
        if lhsT_buf is not None and lhsT_buf.last_write is not None:
            dl.append(lhsT_buf.last_write)
        if rhs_buf is not None and rhs_buf.last_write is not None:
            dl.append(rhs_buf.last_write)
        if start:
            if ps.last_write is not None:
                dl.append(ps.last_write)
            dl.extend(ps.readers)
        self.val["tensor"] += 1
        tok = ("tensor", self.val["tensor"])
        self.prev = tok
        oap = ps.ap if out_ap is None else out_ap
        self.ops["tensor"].append(
            (lambda e, oap=oap, la=lhsT_ap, ra=rhs_ap, st=start, sp=stop:
             e.matmul(oap, la, ra, start=st, stop=sp), dl, 1, False))
        if lhsT_buf is not None:
            lhsT_buf.readers.append(tok)
        if rhs_buf is not None:
            rhs_buf.readers.append(tok)
        if stop:
            ps.last_write = tok
            ps.readers = []
        return tok

    def dma(self, dst_ap, src_ap, reads=(), writes=(), deps=(), eng="sync"):
        dl = list(deps)
        if SER_DMA or (SER_DMA_ST and reads) or (SER_DMA_LD and not reads):
            dl.extend((e2, self.val[e2]) for e2 in self.ENGS if self.val[e2] > 0)
        return self.emit(
            eng, lambda e: e.dma_start(out=dst_ap, in_=src_ap),
            reads=reads, writes=writes, deps=dl, dma=True)

    def finalize(self):
        nc = self.nc
        with nc.Block() as block:
            for eng in self.ENGS:
                if not self.ops[eng]:
                    continue
                def make(eng):
                    def body(e):
                        water = {s: 0 for s in self.ENGS}
                        for fn, deps, inc, psrc in self.ops[eng]:
                            need = {}
                            for (de, dv) in deps:
                                if dv > water[de]:
                                    need[de] = max(need.get(de, 0), dv)
                            waited_pe = False
                            for de, dv in need.items():
                                e.wait_ge(self.sem[de], dv)
                                water[de] = dv
                                if de == "tensor":
                                    waited_pe = True
                            if psrc and waited_pe:
                                e.nop(cycle_cnt=192, nofuse=True)
                            ins = fn(e)
                            ins.then_inc(self.sem[eng], inc)
                    return body
                getattr(block, {"sync": "sync", "vector": "vector", "scalar": "scalar",
                                "gpsimd": "gpsimd", "tensor": "tensor"}[eng])(make(eng))
        self.es.close()


def sbuf(nc, name, shape, dtype):
    return Buf(nc.alloc_sbuf_tensor(name, shape, dtype))

def psum(nc, name, shape, dtype=F32):
    return Buf(nc.alloc_psum_tensor(name, shape, dtype))



def host_fm(x_tok):
    return np.ascontiguousarray(x_tok.T)


def emit_ln(s, C, xblk, xn_out):
    v, a = "vector", "scalar"
    ones_col, ones_row = C["ones_col"], C["ones_row"]
    ps_s1, ps_s2, ps_r, ps_m = C["ps_s1"], C["ps_s2"], C["ps_rstd"], C["ps_mmr"]
    st_mean, st_rstd, st_mmr = C["st_mean"], C["st_rstd"], C["st_mmr"]
    tmp_sq, tmp32 = C["tmp_sq"], C["tmp32"]
    for kb in range(DBLK):
        s.emit(v, lambda e, kb=kb: e.tensor_copy(tmp_sq.ap, xblk[kb].ap),
               reads=[xblk[kb]], writes=[tmp_sq])
        s.mm(ps_s1, ones_col, ones_col.ap, tmp_sq, tmp_sq.ap,
             start=(kb == 0), stop=(kb == DBLK - 1))
        s.emit(a, lambda e, kb=kb: e.activation(C["tmp_sq2"].ap, xblk[kb].ap, AF.Square),
               reads=[xblk[kb]], writes=[C["tmp_sq2"]])
        s.mm(ps_s2, ones_col, ones_col.ap, C["tmp_sq2"], C["tmp_sq2"].ap,
             start=(kb == 0), stop=(kb == DBLK - 1))
    s.emit(v, lambda e: e.tensor_scalar(st_mean.ap, ps_s1.ap, 1.0 / D, None, ALU.mult),
           reads=[ps_s1], writes=[st_mean], psum_src=True)
    s.emit(v, lambda e: e.tensor_mul(st_rstd.ap, st_mean.ap, st_mean.ap),
           reads=[st_mean], writes=[st_rstd])
    s.emit(v, lambda e: e.scalar_tensor_tensor(st_rstd.ap, ps_s2.ap, 1.0 / D,
                                               st_rstd.ap, ALU.mult, ALU.subtract),
           reads=[ps_s2, st_rstd], writes=[st_rstd], psum_src=True)
    s.emit(a, lambda e: e.activation(st_rstd.ap, st_rstd.ap, AF.Ln,
                                     bias=C["eps1"][0:1, 0:1]),
           reads=[st_rstd, C["eps1"]], writes=[st_rstd])
    s.emit(a, lambda e: e.activation(st_rstd.ap, st_rstd.ap, AF.Exp, scale=-0.5),
           reads=[st_rstd], writes=[st_rstd])
    s.emit(v, lambda e: e.tensor_mul(st_mmr.ap, st_mean.ap, st_rstd.ap),
           reads=[st_mean, st_rstd], writes=[st_mmr])
    s.mm(ps_r, ones_row, ones_row.ap, st_rstd, st_rstd.ap, start=True, stop=True)
    s.mm(ps_m, ones_row, ones_row.ap, st_mmr, st_mmr.ap, start=True, stop=True)
    for kb in range(DBLK):
        s.emit(v, lambda e, kb=kb: e.tensor_mul(tmp32.ap, xblk[kb].ap, ps_r.ap),
               reads=[xblk[kb], ps_r], writes=[tmp32], psum_src=True)
        s.emit(v, lambda e, kb=kb: e.tensor_sub(xn_out[kb].ap, tmp32.ap, ps_m.ap),
               reads=[tmp32, ps_m], writes=[xn_out[kb]], psum_src=True)


def emit_pwl(s, nc, ps_h, b1c_ap, b1c_buf, tiles, hb):
    v, a, g = "vector", "scalar", "gpsimd"
    ti, uc, accp, accn, hin = (tiles[k] for k in ("ti", "uc", "accp", "accn", "hin"))
    s.emit(v, lambda e: e.tensor_scalar(ti.ap, ps_h.ap, 256.0, b1c_ap, ALU.mult, ALU.add),
           reads=[ps_h, b1c_buf], writes=[ti], psum_src=True)
    s.emit(v, lambda e: e.scalar_tensor_tensor(uc.ap, ti.ap, -1.0, ti.ap, ALU.mult, ALU.max),
           reads=[ti], writes=[uc])
    s.emit(v, lambda e: e.tensor_scalar(uc.ap, uc.ap, 1024.0, None, ALU.min),
           reads=[uc], writes=[uc])
    s.emit(v, lambda e: e.tensor_scalar(accp.ap, uc.ap, _PWL_G0, None, ALU.mult),
           reads=[uc], writes=[accp])
    s.emit(v, lambda e: e.scalar_tensor_tensor(accp.ap, ti.ap, 0.0, accp.ap, ALU.max, ALU.add),
           reads=[ti, accp], writes=[accp])
    first_neg = True
    for k in range(1, 16):
        dk = _PWL_DJ[k - 1]
        ad = abs(dk)
        s.emit(a, lambda e, ad=ad, k=k: e.activation(
                   hin.ap, uc.ap, AF.Relu,
                   bias=s.C_hingeb[:, k - 1:k], scale=ad),
               reads=[uc, s.C_hingeb_buf], writes=[hin])
        if dk >= 0:
            s.emit(v, lambda e: e.tensor_add(accp.ap, accp.ap, hin.ap),
                   reads=[hin, accp], writes=[accp])
        elif first_neg:
            s.emit(v, lambda e: e.tensor_copy(accn.ap, hin.ap),
                   reads=[hin], writes=[accn])
            first_neg = False
        elif USE_POOL_ACC:
            s.emit(g, lambda e: e.tensor_tensor(accn.ap, accn.ap, hin.ap, ALU.add),
                   reads=[hin, accn], writes=[accn])
        else:
            s.emit(v, lambda e: e.tensor_add(accn.ap, accn.ap, hin.ap),
                   reads=[hin, accn], writes=[accn])
    s.emit(v, lambda e: e.tensor_tensor(ti.ap, accp.ap, accn.ap, ALU.subtract),
           reads=[accp, accn], writes=[ti])
    s.emit(v, lambda e: e.tensor_copy(hb.ap, ti.ap), reads=[ti], writes=[hb])
    return hb


def emit_ffn(s, C, xblk, w2_h):
    v, a, g, t = "vector", "scalar", "gpsimd", "tensor"
    s.C_hingeb = C["hingeb"].t
    s.C_hingeb_buf = C["hingeb"]
    xn = C["xn"]
    if LN_BYPASS:
        for kb in range(DBLK):
            s.emit(v, lambda e, kb=kb: e.tensor_copy(xn[kb].ap, xblk[kb].ap),
                   reads=[xblk[kb]], writes=[xn[kb]])
    else:
        emit_ln(s, C, xblk, xn)
    h1 = C["h1"]
    ps_h = C["ps_h"]
    for m in range(FBLK):
        ph = ps_h[m % 2]
        for kb in range(DBLK):
            s.mm(ph, C["w1b"][kb], C["w1b"][kb][:, m * P:(m + 1) * P],
                 xn[kb], xn[kb].ap, start=(kb == 0), stop=(kb == DBLK - 1))
        tiles = C["pwl"][m % 2]
        if PWL_BYPASS:
            s.emit(v, lambda e, ph=ph, m=m: e.tensor_copy(h1[m].ap, ph.ap),
                   reads=[ph], writes=[h1[m]], psum_src=True)
        else:
            emit_pwl(s, None, ph, C["b1c"][:, m:m + 1], C["b1c_buf"], tiles, h1[m])
    w2r = w2_h.rearrange("(m p) o -> p m o", p=P)
    for mo in range(DBLK):
        w2b = C["w2b"][mo % 2]
        for h in range(2):
            w2s = C["wst"][h]
            hm = FBLK // 2
            s.dma(w2s.ap, w2r[:, h * hm:(h + 1) * hm, mo * P:(mo + 1) * P],
                  writes=[w2s])
            s.emit(v, lambda e, w2b=w2b, w2s=w2s, h=h, hm=hm: e.tensor_copy(
                       w2b[:, h * hm * P:(h + 1) * hm * P], w2s.ap),
                   reads=[w2s], writes=[w2b])
        po = C["ps_o"][mo % 2]
        for m in range(FBLK):
            s.mm(po, w2b, w2b[:, m * P:(m + 1) * P], h1[m], h1[m].ap,
                 start=(m == 0), stop=(m == FBLK - 1))
        t3 = C["t3"][mo % 2]
        s.emit(v, lambda e, t3=t3, po=po, mo=mo: e.tensor_scalar(
                   t3.ap, po.ap, C["b2c"][:, mo:mo + 1], None, ALU.add),
               reads=[po, C["b2c_buf"]], writes=[t3], psum_src=True)
        s.emit(v, lambda e, t3=t3, mo=mo: e.scalar_tensor_tensor(
                   xblk[mo].ap, t3.ap, 1.0 / 512.0, xblk[mo].ap, ALU.mult, ALU.add),
               reads=[t3, xblk[mo]], writes=[xblk[mo]])



def make_common(s, nc):
    C = {}
    C["ones_col"] = sbuf(nc, "ones_col", [P, 1], BF16)
    C["ones_row"] = sbuf(nc, "ones_row", [1, P], F32)
    s.emit("vector", lambda e: e.memset(C["ones_col"].ap, 1.0), writes=[C["ones_col"]])
    s.emit("vector", lambda e: e.memset(C["ones_row"].ap, 1.0), writes=[C["ones_row"]])
    C["ps_s1"] = psum(nc, "ps_s1", [1, TF])
    C["ps_s2"] = psum(nc, "ps_s2", [1, TF])
    C["ps_rstd"] = psum(nc, "ps_rstd", [P, TF])
    C["ps_mmr"] = psum(nc, "ps_mmr", [P, TF])
    C["eps1"] = sbuf(nc, "eps1", [1, 1], F32)
    s.emit("vector", lambda e: e.memset(C["eps1"].ap, EPS), writes=[C["eps1"]])
    C["st_mean"] = sbuf(nc, "st_mean", [1, TF], F32)
    C["st_rstd"] = sbuf(nc, "st_rstd", [1, TF], F32)
    C["st_mmr"] = sbuf(nc, "st_mmr", [1, TF], F32)
    C["tmp_sq"] = sbuf(nc, "tmp_sq", [P, TF], BF16)
    C["tmp_sq2"] = sbuf(nc, "tmp_sq2", [P, TF], BF16)
    C["tmp32"] = sbuf(nc, "tmp32", [P, TF], F32)
    C["xn"] = [sbuf(nc, f"xn{kb}", [P, TF], BF16) for kb in range(DBLK)]
    return C


def make_ffn_bufs(s, nc, C, w1_h, b1_h, b2_h):
    v = "vector"
    C["w1b"] = [sbuf(nc, f"w1b{kb}", [P, FF], BF16) for kb in range(DBLK)]
    C["wst"] = [sbuf(nc, f"wst{i}", [P, FF // 2], F32) for i in range(2)]
    for kb in range(DBLK):
        for h in range(2):
            st = C["wst"][(kb * 2 + h) % 2]
            sl = slice(h * (FF // 2), (h + 1) * (FF // 2))
            s.dma(st.ap, w1_h[kb * P:(kb + 1) * P, sl], writes=[st])
            s.emit(v, lambda e, kb=kb, st=st, sl=sl: e.tensor_copy(
                       C["w1b"][kb][:, sl], st.ap),
                   reads=[st], writes=[C["w1b"][kb]])
    b1r = sbuf(nc, "b1r", [P, FBLK], F32)
    s.dma(b1r.ap, b1_h[:], writes=[b1r])
    b1c = sbuf(nc, "b1c", [P, FBLK], F32)
    s.emit(v, lambda e: e.tensor_scalar(b1c.ap, b1r.ap, 256.0, None, ALU.mult),
           reads=[b1r], writes=[b1c])
    C["b1c"], C["b1c_buf"] = b1c.t, b1c
    b2r = sbuf(nc, "b2r", [P, DBLK], F32)
    s.dma(b2r.ap, b2_h[:], writes=[b2r])
    b2c = sbuf(nc, "b2c", [P, DBLK], F32)
    s.emit(v, lambda e: e.tensor_scalar(b2c.ap, b2r.ap, 256.0, None, ALU.mult),
           reads=[b2r], writes=[b2c])
    C["b2c"], C["b2c_buf"] = b2c.t, b2c
    C["h1"] = [sbuf(nc, f"h1_{m}", [P, TF], BF16) for m in range(FBLK)]
    C["ps_h"] = [psum(nc, f"ps_h{i}", [P, TF]) for i in range(2)]
    C["ps_o"] = [psum(nc, f"ps_o{i}", [P, TF]) for i in range(2)]
    C["w2b"] = [sbuf(nc, f"w2b{i}", [P, FF], BF16) for i in range(2)]
    C["t3"] = [sbuf(nc, f"t3_{i}", [P, TF], I32) for i in range(2)]
    hb = sbuf(nc, "hingeb", [P, 15], F32)
    for k in range(1, 16):
        ad = abs(_PWL_DJ[k - 1])
        s.emit(v, lambda e, k=k, ad=ad: e.memset(hb[:, k - 1:k], -ad * 64.0 * k),
               writes=[hb])
    C["hingeb"] = hb
    C["pwl"] = []
    for i in range(2):
        C["pwl"].append(dict(
            ti=sbuf(nc, f"pw_ti{i}", [P, TF], I32),
            uc=sbuf(nc, f"pw_uc{i}", [P, TF], F32),
            accp=sbuf(nc, f"pw_ap{i}", [P, TF], F32),
            accn=sbuf(nc, f"pw_an{i}", [P, TF], F32),
            hin=sbuf(nc, f"pw_hi{i}", [P, TF], F32),
        ))
    return C


def build_p1():
    nc = bass.Bass("TRN2", target_bir_lowering=False, debug=False)
    x_h = nc.dram_tensor("x", [D, TC], F32, kind="ExternalInput")
    w1_h = nc.dram_tensor("w1", [D, FF], F32, kind="ExternalInput")
    b1_h = nc.dram_tensor("b1", [P, FBLK], F32, kind="ExternalInput")
    w2_h = nc.dram_tensor("w2", [FF, D], F32, kind="ExternalInput")
    b2_h = nc.dram_tensor("b2", [P, DBLK], F32, kind="ExternalInput")
    wz_h = nc.dram_tensor("wz", [D, D], F32, kind="ExternalInput")
    wx_h = nc.dram_tensor("wx", [D, D], F32, kind="ExternalInput")
    wdt_h = nc.dram_tensor("wdt", [D, D], F32, kind="ExternalInput")
    bz_h = nc.dram_tensor("bz", [P, DBLK], F32, kind="ExternalInput")
    bx_h = nc.dram_tensor("bx", [P, DBLK], F32, kind="ExternalInput")
    bdt_h = nc.dram_tensor("bdt", [P, DBLK], F32, kind="ExternalInput")
    wb_h = nc.dram_tensor("wb", [D, NSTATE], F32, kind="ExternalInput")
    wc_h = nc.dram_tensor("wc", [D, NSTATE], F32, kind="ExternalInput")
    bb_h = nc.dram_tensor("bb", [NSTATE, 1], F32, kind="ExternalInput")
    bc_h = nc.dram_tensor("bc", [NSTATE, 1], F32, kind="ExternalInput")
    x2_h = nc.dram_tensor("x2", [D, TC], F32, kind="ExternalOutput")
    z_h = nc.dram_tensor("z", [D, TC], BF16, kind="ExternalOutput")
    dt_h = nc.dram_tensor("dt", [D, TC], F32, kind="ExternalOutput")
    vv_h = nc.dram_tensor("vv", [D, TC], F32, kind="ExternalOutput")
    bm_h = nc.dram_tensor("bm", [NSTATE, TC], F32, kind="ExternalOutput")
    cc_h = nc.dram_tensor("cc", [NSTATE, TC], F32, kind="ExternalOutput")

    s = Sch(nc)
    v, a, g, t = "vector", "scalar", "gpsimd", "tensor"
    C = make_common(s, nc)
    C = make_ffn_bufs(s, nc, C, w1_h, b1_h, b2_h)
    x2_d = Buf(x2_h)

    xs = [sbuf(nc, f"xs{kb}", [P, TF], F32) for kb in range(DBLK)]

    for f in range(NT if P1_NT is None else P1_NT):
        sl = slice(f * TF, (f + 1) * TF)
        for kb in range(DBLK):
            s.dma(xs[kb].ap, x_h[kb * P:(kb + 1) * P, sl], writes=[xs[kb]])
        emit_ffn(s, C, xs, w2_h)
        for kb in range(DBLK):
            s.dma(x2_h[kb * P:(kb + 1) * P, sl], xs[kb].ap, reads=[xs[kb]], writes=[x2_d])

    pbias = {}
    for name, h in (("bz", bz_h), ("bx", bx_h), ("bdt", bdt_h)):
        bb = sbuf(nc, f"{name}c", [P, DBLK], F32)
        s.dma(bb.ap, h[:], writes=[bb])
        pbias[name] = bb
    bb16 = sbuf(nc, "bbc", [NSTATE, 1], F32)
    s.dma(bb16.ap, bb_h[:], writes=[bb16])
    bc16 = sbuf(nc, "bcc", [NSTATE, 1], F32)
    s.dma(bc16.ap, bc_h[:], writes=[bc16])

    wofs = {"wz": 0, "wx": D, "wdt": 2 * D}
    wres = {}
    for name, h in (("wz", wz_h), ("wx", wx_h), ("wdt", wdt_h)):
        off = wofs[name]
        for kb in range(DBLK):
            st = C["wst"][kb % 2]
            s.dma(st[:, 0:D], h[kb * P:(kb + 1) * P, :], writes=[st])
            s.emit(v, lambda e, kb=kb, st=st, off=off: e.tensor_copy(
                       C["w1b"][kb][:, off:off + D], st[:, 0:D]),
                   reads=[st], writes=[C["w1b"][kb]])
        wres[name] = [(C["w1b"][kb], off) for kb in range(DBLK)]
    for name, h in (("wb", wb_h), ("wc", wc_h)):
        st = C["wst"][0]
        s.dma(st[:, 0:DBLK * NSTATE],
              h.rearrange("(k p) n -> p k n", p=P), writes=[st])
        wb = sbuf(nc, f"{name}b", [P, DBLK * NSTATE], BF16)
        s.emit(v, lambda e, wb=wb, st=st: e.tensor_copy(wb.ap, st[:, 0:DBLK * NSTATE]),
               reads=[st], writes=[wb])
        wres[name] = wb

    xssm = [sbuf(nc, f"xssm{i}", [P, TF], F32) for i in range(2)]
    dtt = [sbuf(nc, f"dtt{i}", [P, TF], F32) for i in range(2)]
    pb = [sbuf(nc, f"pb{i}", [P, TF], F32) for i in range(2)]
    sc1 = [sbuf(nc, f"sc1_{i}", [P, TF], F32) for i in range(2)]
    zt = sbuf(nc, "zt", [P, TF], BF16)
    ps_bc = C["ps_rstd"]
    sm16 = sbuf(nc, "sm16", [NSTATE, TF], F32)

    for f in range((NT if P1_NT is None else P1_NT) if P1_PROJ else 0):
        sl = slice(f * TF, (f + 1) * TF)
        for kb in range(DBLK):
            s.dma(xs[kb].ap, x2_h[kb * P:(kb + 1) * P, sl], reads=[x2_d], writes=[xs[kb]])
        xn = C["xn"]
        emit_ln(s, C, xs, xn)
        for mo in range(DBLK):
            po = C["ps_o"][mo % 2]
            for kb in range(DBLK):
                wbuf, off = wres["wz"][kb]
                s.mm(po, wbuf, wbuf[:, off + mo * P:off + (mo + 1) * P],
                     xn[kb], xn[kb].ap, start=(kb == 0), stop=(kb == DBLK - 1))
            s.emit(a, lambda e, po=po, mo=mo: e.activation(
                       zt.ap, po.ap, AF.Sigmoid, bias=pbias["bz"][:, mo:mo + 1]),
                   reads=[po, pbias["bz"]], writes=[zt], psum_src=True)
            s.dma(z_h[mo * P:(mo + 1) * P, sl], zt.ap, reads=[zt])
        for mo in range(DBLK):
            po = C["ps_o"][mo % 2]
            for kb in range(DBLK):
                wbuf, off = wres["wdt"][kb]
                s.mm(po, wbuf, wbuf[:, off + mo * P:off + (mo + 1) * P],
                     xn[kb], xn[kb].ap, start=(kb == 0), stop=(kb == DBLK - 1))
            i2 = mo % 2
            s.emit(a, lambda e, po=po, mo=mo, i2=i2: e.activation(
                       pb[i2].ap, po.ap, AF.Identity, bias=pbias["bdt"][:, mo:mo + 1]),
                   reads=[po, pbias["bdt"]], writes=[pb[i2]], psum_src=True)
            s.emit(v, lambda e, i2=i2: e.scalar_tensor_tensor(
                       sc1[i2].ap, pb[i2].ap, -1.0, pb[i2].ap, ALU.mult, ALU.max),
                   reads=[pb[i2]], writes=[sc1[i2]])
            s.emit(a, lambda e, i2=i2: e.activation(sc1[i2].ap, sc1[i2].ap, AF.Exp,
                                                    scale=-1.0),
                   reads=[sc1[i2]], writes=[sc1[i2]])
            s.emit(a, lambda e, i2=i2: e.activation(sc1[i2].ap, sc1[i2].ap, AF.Ln,
                                                    bias=1.0),
                   reads=[sc1[i2]], writes=[sc1[i2]])
            s.emit(v, lambda e, i2=i2: e.scalar_tensor_tensor(
                       dtt[i2].ap, pb[i2].ap, 0.0, sc1[i2].ap, ALU.max, ALU.add),
                   reads=[pb[i2], sc1[i2]], writes=[dtt[i2]])
            s.dma(dt_h[mo * P:(mo + 1) * P, sl], dtt[i2].ap, reads=[dtt[i2]])
            po2 = C["ps_h"][mo % 2]
            for kb in range(DBLK):
                wbuf, off = wres["wx"][kb]
                s.mm(po2, wbuf, wbuf[:, off + mo * P:off + (mo + 1) * P],
                     xn[kb], xn[kb].ap, start=(kb == 0), stop=(kb == DBLK - 1))
            s.emit(a, lambda e, po2=po2, mo=mo, i2=i2: e.activation(
                       xssm[i2].ap, po2.ap, AF.Identity, bias=pbias["bx"][:, mo:mo + 1]),
                   reads=[po2, pbias["bx"]], writes=[xssm[i2]], psum_src=True)
            s.emit(v, lambda e, i2=i2: e.tensor_mul(xssm[i2].ap, xssm[i2].ap, dtt[i2].ap),
                   reads=[dtt[i2], xssm[i2]], writes=[xssm[i2]])
            s.dma(vv_h[mo * P:(mo + 1) * P, sl], xssm[i2].ap, reads=[xssm[i2]])
        for name, out_h, bias in (("wb", bm_h, bb16), ("wc", cc_h, bc16)):
            for kb in range(DBLK):
                s.mm(ps_bc, wres[name], wres[name][:, kb * NSTATE:(kb + 1) * NSTATE],
                     xn[kb], xn[kb].ap, start=(kb == 0), stop=(kb == DBLK - 1),
                     out_ap=ps_bc.t[0:NSTATE, :])
            s.emit(a, lambda e, bias=bias: e.activation(
                       sm16.ap, ps_bc.t[0:NSTATE, :], AF.Identity, bias=bias[:, 0:1]),
                   reads=[ps_bc, bias], writes=[sm16], psum_src=True)
            s.dma(out_h[:, sl], sm16.ap, reads=[sm16])

    s.emit("sync", lambda e: e.nop(), deps=[("sync", s.val["sync"])])
    s.finalize()
    return nc



def build_p2():
    nc = bass.Bass("TRN2", target_bir_lowering=False, debug=False)
    TP2 = 2048
    NCH = L // TP2
    dt_h = [nc.dram_tensor(f"dt{b}", [P, L], F32, kind="ExternalInput") for b in range(B)]
    vv_h = [nc.dram_tensor(f"vv{b}", [P, L], F32, kind="ExternalInput") for b in range(B)]
    bm_h = [nc.dram_tensor(f"bm{b}", [NSTATE, L], F32, kind="ExternalInput") for b in range(B)]
    cc_h = [nc.dram_tensor(f"cm{b}", [NSTATE, L], F32, kind="ExternalInput") for b in range(B)]
    al_h = nc.dram_tensor("alog", [P, NSTATE], F32, kind="ExternalInput")
    y_h = [nc.dram_tensor(f"y{b}", [P, L], BF16, kind="ExternalOutput") for b in range(B)]

    s = Sch(nc)
    v, a, g, t = "vector", "scalar", "gpsimd", "tensor"

    dtb = sbuf(nc, "dtb", [P, L], F32)
    vb = sbuf(nc, "vb", [P, L], F32)
    acol = sbuf(nc, "acol", [P, NSTATE], F32)
    onesf = sbuf(nc, "onesf", [1, P], F32)
    rowB = [sbuf(nc, f"rowB{i}", [1, TP2], F32) for i in range(2)]
    rowC = [sbuf(nc, f"rowC{i}", [1, TP2], F32) for i in range(2)]
    NPB = TP2 // TF
    ps_B = [psum(nc, f"ps_B{i}", [P, TF]) for i in range(NPB)]
    ps_C = [psum(nc, f"ps_C{i}", [P, TF]) for i in range(NPB)]
    at = [sbuf(nc, f"at{i}", [P, TP2], F32) for i in range(2)]
    ut = [sbuf(nc, f"ut{i}", [P, TP2], F32) for i in range(2)]
    ht = [sbuf(nc, f"ht{i}", [P, TP2], F32) for i in range(2)]
    qt = sbuf(nc, "qt", [P, TP2], F32)
    y32 = sbuf(nc, "y32", [P, TP2], F32)
    yb = sbuf(nc, "yb", [P, TP2], BF16)
    carry = sbuf(nc, "carry", [P, NSTATE], F32)

    s.emit(v, lambda e: e.memset(onesf.ap, 1.0), writes=[onesf])
    s.dma(acol.ap, al_h[:], writes=[acol])
    s.emit(a, lambda e: e.activation(acol.ap, acol.ap, AF.Exp), reads=[acol], writes=[acol])
    s.emit(v, lambda e: e.tensor_scalar(acol.ap, acol.ap, -1.0, None, ALU.mult),
           reads=[acol], writes=[acol])

    for b in range(B):
        s.dma(dtb.ap, dt_h[b][:], writes=[dtb])
        s.dma(vb.ap, vv_h[b][:], writes=[vb])
        s.emit(v, lambda e: e.memset(carry.ap, 0.0), writes=[carry])
        for c in range(NCH):
            sl = slice(c * TP2, (c + 1) * TP2)
            for n in range(NSTATE):
                i2 = n % 2
                rB, rC = rowB[i2], rowC[i2]
                s.dma(rB.ap, bm_h[b][n:n + 1, sl], writes=[rB])
                s.dma(rC.ap, cc_h[b][n:n + 1, sl], writes=[rC])
                for p in range(NPB):
                    s.mm(ps_B[p], onesf, onesf.ap, rB, rB[0:1, p * TF:(p + 1) * TF],
                         start=True, stop=True)
                    s.mm(ps_C[p], onesf, onesf.ap, rC, rC[0:1, p * TF:(p + 1) * TF],
                         start=True, stop=True)
                s.emit(a, lambda e, i2=i2, n=n, sl=sl: e.activation(
                           at[i2].ap, dtb[:, sl], AF.Exp, scale=acol[:, n:n + 1]),
                       reads=[dtb, acol], writes=[at[i2]])
                for p in range(NPB):
                    s.emit(v, lambda e, i2=i2, sl=sl, p=p: e.tensor_tensor(
                               ut[i2][:, p * TF:(p + 1) * TF],
                               vb[:, sl.start + p * TF:sl.start + (p + 1) * TF],
                               ps_B[p].ap, ALU.mult),
                           reads=[vb, ps_B[p]], writes=[ut[i2]], psum_src=True)
                s.emit(v, lambda e, i2=i2, n=n: e.tensor_tensor_scan(
                           ht[i2].ap, at[i2].ap, ut[i2].ap, carry[:, n:n + 1],
                           ALU.mult, ALU.add),
                       reads=[at[i2], ut[i2], carry], writes=[ht[i2]])
                s.emit(v, lambda e, i2=i2, n=n: e.tensor_copy(
                           carry[:, n:n + 1], ht[i2][:, TP2 - 1:TP2]),
                       reads=[ht[i2]], writes=[carry])
                for p in range(NPB):
                    s.emit(v, lambda e, i2=i2, p=p: e.tensor_tensor(
                               qt[:, p * TF:(p + 1) * TF],
                               ht[i2][:, p * TF:(p + 1) * TF], ps_C[p].ap, ALU.mult),
                           reads=[ht[i2], ps_C[p]], writes=[qt], psum_src=True)
                if n == 0:
                    s.emit(v, lambda e: e.tensor_copy(y32.ap, qt.ap),
                           reads=[qt], writes=[y32])
                else:
                    s.emit(v, lambda e: e.tensor_add(y32.ap, y32.ap, qt.ap),
                           reads=[qt, y32], writes=[y32])
            s.emit(v, lambda e: e.tensor_copy(yb.ap, y32.ap), reads=[y32], writes=[yb])
            s.dma(y_h[b][:, sl], yb.ap, reads=[yb])
    s.emit("sync", lambda e: e.nop(), deps=[("sync", s.val["sync"])])
    s.finalize()
    return nc


def p2_inmaps(p1res):
    dt_all = [np.asarray(r["dt"], np.float32) for r in p1res]
    vv_all = [np.asarray(r["vv"], np.float32) for r in p1res]
    bm_all = [np.asarray(r["bm"], np.float32) for r in p1res]
    cc_all = [np.asarray(r["cc"], np.float32) for r in p1res]
    alog = p1res[0]["_alog"]
    maps = []
    for j in range(NCORES):
        m = {}
        for b in range(B):
            cores = range(b * 4, b * 4 + 4)
            m[f"dt{b}"] = np.ascontiguousarray(
                np.concatenate([dt_all[i][j * P:(j + 1) * P, :] for i in cores], axis=1))
            m[f"vv{b}"] = np.ascontiguousarray(
                np.concatenate([vv_all[i][j * P:(j + 1) * P, :] for i in cores], axis=1))
            m[f"bm{b}"] = np.ascontiguousarray(
                np.concatenate([bm_all[i] for i in cores], axis=1))
            m[f"cm{b}"] = np.ascontiguousarray(
                np.concatenate([cc_all[i] for i in cores], axis=1))
        m["alog"] = np.ascontiguousarray(alog[j * P:(j + 1) * P, :])
        maps.append(m)
    return maps



def build_p3():
    nc = bass.Bass("TRN2", target_bir_lowering=False, debug=False)
    x2_h = nc.dram_tensor("x2", [D, TC], F32, kind="ExternalInput")
    y_h = nc.dram_tensor("y", [D, TC], BF16, kind="ExternalInput")
    z_h = nc.dram_tensor("z", [D, TC], BF16, kind="ExternalInput")
    wo_h = nc.dram_tensor("wo", [D, D], F32, kind="ExternalInput")
    bo_h = nc.dram_tensor("bo", [P, DBLK], F32, kind="ExternalInput")
    w1_h = nc.dram_tensor("w1", [D, FF], F32, kind="ExternalInput")
    b1_h = nc.dram_tensor("b1", [P, FBLK], F32, kind="ExternalInput")
    w2_h = nc.dram_tensor("w2", [FF, D], F32, kind="ExternalInput")
    b2_h = nc.dram_tensor("b2", [P, DBLK], F32, kind="ExternalInput")
    xo_h = nc.dram_tensor("xo", [D, TC], F32, kind="ExternalOutput")

    s = Sch(nc)
    v, a, g, t = "vector", "scalar", "gpsimd", "tensor"
    C = make_common(s, nc)
    C = make_ffn_bufs(s, nc, C, w1_h, b1_h, b2_h)

    boc = sbuf(nc, "boc", [P, DBLK], F32)
    s.dma(boc.ap, bo_h[:], writes=[boc])

    xs = [sbuf(nc, f"xs{kb}", [P, TF], F32) for kb in range(DBLK)]
    gt = C["h1"][:DBLK]
    zt = sbuf(nc, "zt3", [P, TF], BF16)
    yt = sbuf(nc, "yt3", [P, TF], BF16)
    xo_d = Buf(xo_h)

    for f in range(NT):
        sl = slice(f * TF, (f + 1) * TF)
        for kb in range(DBLK):
            s.dma(xs[kb].ap, x2_h[kb * P:(kb + 1) * P, sl], writes=[xs[kb]])
        wob = C["w2b"][0]
        for kb in range(DBLK):
            st = C["wst"][kb % 2]
            s.dma(st[:, 0:D], wo_h[kb * P:(kb + 1) * P, :], writes=[st])
            s.emit(v, lambda e, kb=kb, st=st, wob=wob: e.tensor_copy(
                       wob[:, kb * 512:kb * 512 + 512], st[:, 0:512]),
                   reads=[st], writes=[wob])
            s.emit(v, lambda e, kb=kb, st=st, wob=wob: e.tensor_copy(
                       C["w2b"][1][:, kb * 512:kb * 512 + 512], st[:, 512:D]),
                   reads=[st], writes=[C["w2b"][1]])
        for kb in range(DBLK):
            s.dma(yt.ap, y_h[kb * P:(kb + 1) * P, sl], writes=[yt])
            s.dma(zt.ap, z_h[kb * P:(kb + 1) * P, sl], writes=[zt])
            s.emit(v, lambda e, kb=kb: e.tensor_tensor(gt[kb].ap, yt.ap, zt.ap, ALU.mult),
                   reads=[yt, zt], writes=[gt[kb]])
        for mo in range(DBLK):
            po = C["ps_o"][mo % 2]
            for kb in range(DBLK):
                half, off = divmod(mo * P, 512)
                wob2 = C["w2b"][half]
                s.mm(po, wob2, wob2[:, kb * 512 + off:kb * 512 + off + P],
                     gt[kb], gt[kb].ap, start=(kb == 0), stop=(kb == DBLK - 1))
            s.emit(v, lambda e, po=po, mo=mo: e.scalar_tensor_tensor(
                       xs[mo].ap, po.ap, boc[:, mo:mo + 1], xs[mo].ap, ALU.add, ALU.add),
                   reads=[po, boc, xs[mo]], writes=[xs[mo]], psum_src=True)
        emit_ffn(s, C, xs, w2_h)
        for kb in range(DBLK):
            s.dma(xo_h[kb * P:(kb + 1) * P, sl], xs[kb].ap, reads=[xs[kb]], writes=[xo_d])
    s.emit("sync", lambda e: e.nop(), deps=[("sync", s.val["sync"])])
    s.finalize()
    return nc


def p3_inmaps(inp, p1res, p2res):
    maps = []
    shared = dict(
        wo=np.ascontiguousarray(np.asarray(inp["L0_attn_Wout"], np.float32)),
        bo=_col(np.asarray(inp["L0_attn_bout"], np.float32)),
        w1=np.ascontiguousarray(np.asarray(inp["L0_ffn_w1"], np.float32)),
        b1=_col(np.asarray(inp["L0_ffn_b1"], np.float32)),
        w2=np.ascontiguousarray(np.asarray(inp["L0_ffn_w2"], np.float32)),
        b2=_col(np.asarray(inp["L0_ffn_b2"], np.float32)),
    )
    for i in range(NCORES):
        b, qq = divmod(i, 4)
        ysl = slice(qq * TC, (qq + 1) * TC)
        yfm = np.concatenate([np.asarray(p2res[j][f"y{b}"])[:, ysl] for j in range(NCORES)], axis=0)
        m = dict(shared)
        m["x2"] = np.asarray(p1res[i]["x2"], np.float32)
        m["z"] = np.asarray(p1res[i]["z"])
        m["y"] = np.ascontiguousarray(yfm)
        maps.append(m)
    return maps


def kernel(**inputs):
    nc1 = build_p1()
    maps1 = p1_inmaps(inputs)
    r1 = run_bass_kernel_spmd(nc1, maps1, core_ids=list(range(NCORES)))
    p1res = [dict(r) for r in r1.results]
    for r in p1res:
        r["_alog"] = np.asarray(inputs["L0_attn_A_log"], np.float32)

    nc2 = build_p2()
    maps2 = p2_inmaps(p1res)
    r2 = run_bass_kernel_spmd(nc2, maps2, core_ids=list(range(NCORES)))
    p2res = [dict(r) for r in r2.results]

    nc3 = build_p3()
    maps3 = p3_inmaps(inputs, p1res, p2res)
    r3 = run_bass_kernel_spmd(nc3, maps3, core_ids=list(range(NCORES)))

    out = np.empty((B * L, D), np.float32)
    for i in range(NCORES):
        out[i * TC:(i + 1) * TC, :] = np.asarray(r3.results[i]["xo"], np.float32).T
    return out.reshape(B, L, D)


def _col(b):
    n = b.shape[0] // P
    return np.ascontiguousarray(b.reshape(n, P).T)


def p1_inmaps(inp):
    xf = np.asarray(inp["L0_attn_x"], np.float32).reshape(B * L, D)
    shared = dict(
        w1=np.ascontiguousarray(np.asarray(inp["L0_ffn_w1"], np.float32)),
        b1=_col(np.asarray(inp["L0_ffn_b1"], np.float32)),
        w2=np.ascontiguousarray(np.asarray(inp["L0_ffn_w2"], np.float32)),
        b2=_col(np.asarray(inp["L0_ffn_b2"], np.float32)),
        wz=np.ascontiguousarray(np.asarray(inp["L0_attn_Wz"], np.float32)),
        wx=np.ascontiguousarray(np.asarray(inp["L0_attn_Wx"], np.float32)),
        wdt=np.ascontiguousarray(np.asarray(inp["L0_attn_Wdt"], np.float32)),
        bz=_col(np.asarray(inp["L0_attn_bz"], np.float32)),
        bx=_col(np.asarray(inp["L0_attn_bx"], np.float32)),
        bdt=_col(np.asarray(inp["L0_attn_bdt"], np.float32)),
        wb=np.ascontiguousarray(np.asarray(inp["L0_attn_WB"], np.float32)),
        wc=np.ascontiguousarray(np.asarray(inp["L0_attn_WC"], np.float32)),
        bb=np.ascontiguousarray(np.asarray(inp["L0_attn_bB"], np.float32).reshape(NSTATE, 1)),
        bc=np.ascontiguousarray(np.asarray(inp["L0_attn_bC"], np.float32).reshape(NSTATE, 1)),
    )
    maps = []
    for i in range(NCORES):
        m = dict(shared)
        m["x"] = np.ascontiguousarray(xf[i * TC:(i + 1) * TC, :].T)
        maps.append(m)
    return maps
